# version 03
# baseline (speedup 1.0000x reference)
import math
from contextlib import ExitStack

import numpy as np
import ml_dtypes

import concourse.bass as bass
import concourse.mybir as mybir
from concourse.bass_utils import run_bass_kernel_spmd

BFD = ml_dtypes.bfloat16
F32 = mybir.dt.float32
BF16 = mybir.dt.bfloat16
AF = mybir.ActivationFunctionType
OP = mybir.AluOpType
AX = mybir.AxisListType

P = 128


class CFG:
    def __init__(self, T=16384, D=2048, NH=16, INTER=1024, NE=8, NZ=4):
        self.T = T
        self.D = D
        self.NH = NH
        self.HD = 128
        self.INTER = INTER
        self.NE = NE
        self.NZ = NZ
        self.TQ = T // 4
        self.NHC = NH // 4
        self.DC = D // P
        self.HCOLS = self.NHC * self.HD
        self.NW = T // P
        self.IC = INTER // P
        assert D % P == 0 and T % 512 == 0 and INTER % P == 0


class Prog:

    def __init__(self, trn="TRN2"):
        self.nc = bass.Bass(trn, target_bir_lowering=False)
        self.es = ExitStack()
        self._chan = {}
        self._consts = {}

    def const(self, val):
        if val not in self._consts:
            t = self.sbuf(f"cst{len(self._consts)}", (P, 1), F32)
            self._consts[val] = t
        return self._consts[val]

    def emit_consts(self, v):
        self.chan("ini")
        last = None
        for val, t in self._consts.items():
            last = v.memset(t[:, :], float(val))
        if last is None:
            last = v.engine_nop()
        self.inc(last, "ini")

    def sbuf(self, name, shape, dt):
        return self.es.enter_context(self.nc.sbuf_tensor(name, list(shape), dt))

    def psum(self, name, shape, dt=F32):
        return self.es.enter_context(self.nc.psum_tensor(name, list(shape), dt))

    def chan(self, name, step=1):
        if name not in self._chan:
            h = self.es.enter_context(self.nc.semaphore(name))
            self._chan[name] = [h, step, 0]
        return name

    def inc(self, instr, name):
        h, step, _ = self._chan[name]
        if instr is not None:
            instr.then_inc(h, step)
        self._chan[name][2] += step
        return instr

    def reset_counts(self):
        for v in self._chan.values():
            v[2] = 0

    def count(self, name):
        return self._chan[name][2]

    def wait(self, eng, name, val=None):
        h, _, n = self._chan[name]
        eng.wait_ge(h, n if val is None else val)

    def block(self):
        return self.es.enter_context(self.nc.Block())

    def close(self):
        self.es.close()



def build_l1(cfg):
    p = Prog()
    nc = p.nc
    TQ, D = cfg.TQ, cfg.D
    NT = TQ // P
    x = nc.dram_tensor("x_tok", [TQ, D], F32, kind="ExternalInput").ap()
    xn = nc.dram_tensor("xn_tok", [TQ, D], BF16, kind="ExternalOutput").ap()

    xt = [p.sbuf(f"xt{i}", (P, D), F32) for i in range(2)]
    xo = [p.sbuf(f"xo{i}", (P, D), BF16) for i in range(2)]
    waste = p.sbuf("waste", (P, D), F32)
    s1 = [p.sbuf(f"s1_{i}", (P, 1), F32) for i in range(2)]
    s2 = [p.sbuf(f"s2_{i}", (P, 1), F32) for i in range(2)]
    mean = [p.sbuf(f"mean{i}", (P, 1), F32) for i in range(2)]
    nm2 = [p.sbuf(f"nm2_{i}", (P, 1), F32) for i in range(2)]
    var = [p.sbuf(f"var{i}", (P, 1), F32) for i in range(2)]
    sstd = [p.sbuf(f"sstd{i}", (P, 1), F32) for i in range(2)]
    rstd = [p.sbuf(f"rstd{i}", (P, 1), F32) for i in range(2)]

    ld = [p.chan("ld0", 16), p.chan("ld1", 16)]
    st = [p.chan("st0", 16), p.chan("st1", 16)]
    ac = p.chan("ac")
    dv = p.chan("dv")

    ldn = [0] * NT
    acn = [0] * NT
    dvn = [0] * NT
    stn = [0] * NT

    EPS = p.const(1e-5)
    ini = p.chan("ini")

    def schedule(E):
        s, sc, v, g = E["s"], E["sc"], E["v"], E["g"]
        p.wait(sc, "ini", 1)
        for i in range(NT):
            j = i % 2
            if i >= 2:
                p.wait(s, dv, dvn[i - 2])
            p.inc(s.dma_start(out=xt[j][:, :], in_=x[i * P:(i + 1) * P, :]), ld[j])
            ldn[i] = p.count(ld[j])
            p.wait(sc, ld[j], ldn[i])
            if i >= 2:
                p.wait(sc, dv, dvn[i - 2])
            t = xt[j]
            sc.activation(waste[:, :], t[:, :], AF.Identity, accum_out=s1[j][:, :])
            sc.drain()
            sc.activation(waste[:, :], t[:, :], AF.Square, accum_out=s2[j][:, :])
            sc.drain()
            sc.mul(mean[j][:, :], s1[j][:, :], 1.0 / D)
            sc.drain()
            sc.activation(nm2[j][:, :], mean[j][:, :], AF.Square)
            sc.drain()
            sc.mul(nm2[j][:, :], nm2[j][:, :], -1.0)
            sc.mul(var[j][:, :], s2[j][:, :], 1.0 / D)
            sc.drain()
            sc.activation(var[j][:, :], var[j][:, :], AF.Identity, bias=nm2[j][:, :])
            sc.drain()
            p.inc(sc.activation(sstd[j][:, :], var[j][:, :], AF.Sqrt,
                                bias=EPS[:, :]), ac)
            acn[i] = p.count(ac)
            p.wait(v, ac, acn[i])
            if i >= 2:
                p.wait(v, st[j], stn[i - 2])
            v.reciprocal(rstd[j][:, :], sstd[j][:, :])
            v.drain()
            p.inc(v.tensor_scalar(xo[j][:, :], xt[j][:, :], mean[j][:, :],
                                  rstd[j][:, :], op0=OP.subtract, op1=OP.mult), dv)
            dvn[i] = p.count(dv)
            p.wait(g, dv, dvn[i])
            p.inc(g.dma_start(out=xn[i * P:(i + 1) * P, :], in_=xo[j][:, :]), st[j])
            stn[i] = p.count(st[j])
        p.wait(g, st[0], p.count("st0"))
        p.wait(g, st[1], p.count("st1"))

    blk = p.block()
    if True:
        @blk.sync
        def _(se):
            p.reset_counts()
            schedule({"s": se, "sc": DryEng(), "v": DryEng(), "g": DryEng()})

        @blk.scalar
        def _(sce):
            p.reset_counts()
            schedule({"s": DryEng(), "sc": sce, "v": DryEng(), "g": DryEng()})

        @blk.vector
        def _(ve):
            p.reset_counts()
            p.emit_consts(ve)
            schedule({"s": DryEng(), "sc": DryEng(), "v": ve, "g": DryEng()})

        @blk.gpsimd
        def _(ge):
            p.reset_counts()
            schedule({"s": DryEng(), "sc": DryEng(), "v": DryEng(), "g": ge})

    p.close()
    return p.nc


class DryEng:
    def __getattr__(self, name):
        return lambda *a, **k: None


def _is_dry(eng):
    return isinstance(eng, DryEng)



def build_l2(cfg):
    p = Prog()
    nc = p.nc
    D, T, NHC, DC = cfg.D, cfg.T, cfg.NHC, cfg.DC
    HCOLS = cfg.HCOLS
    W = 512
    NWIN = T // W

    xn = nc.dram_tensor("xn_fm", [D, T], BF16, kind="ExternalInput").ap()
    wdr = {m: nc.dram_tensor(f"w{m}", [D, HCOLS], F32, kind="ExternalInput").ap()
           for m in "qkvg"}
    wb = nc.dram_tensor("wb", [D, NHC], F32, kind="ExternalInput").ap()
    cw = {m: nc.dram_tensor(f"cw{m}", [HCOLS, 4], F32, kind="ExternalInput").ap()
          for m in "qkv"}
    ecols_d = nc.dram_tensor("e_cols", [P, NHC * NHC], F32, kind="ExternalInput").ap()
    rowsel_d = nc.dram_tensor("rowsel", [NHC, NHC * P], F32, kind="ExternalInput").ap()
    out = {m: nc.dram_tensor(n, [HCOLS, T], BF16, kind="ExternalOutput").ap()
           for m, n in zip("qkvg", ["qc_d", "kh_d", "vc_d", "g_d"])}
    beta_o = nc.dram_tensor("beta_d", [NHC, T], F32, kind="ExternalOutput").ap()
    kapq_o = nc.dram_tensor("kapq_d", [NHC, T], F32, kind="ExternalOutput").ap()

    ws = {m: p.sbuf(f"ws_{m}", (P, DC * HCOLS), BF16) for m in "qkvg"}
    wbs = p.sbuf("wbs", (P, DC * NHC), BF16)
    cws = {m: p.sbuf(f"cws_{m}", (P, 4 * NHC), F32) for m in "qkv"}
    ecols = p.sbuf("ecols", (P, NHC * NHC), BF16)
    rsel = p.sbuf("rsel", (NHC, NHC * P), F32)
    xnt = [[p.sbuf(f"xnt{b}_{d}", (P, W), BF16) for d in range(DC)] for b in range(2)]
    cin = {m: [p.sbuf(f"cin_{m}{h}", (P, W + 4), F32) for h in range(NHC)] for m in "qkv"}
    cs = [[p.sbuf(f"cs{e}_{i}", (P, W), F32) for i in range(2)] for e in range(2)]
    cout = {m: [[p.sbuf(f"cout_{m}{h}_{j}", (P, W), BF16) for j in range(2)]
                for h in range(NHC)] for m in "qkv"}
    convf = {m: [p.sbuf(f"convf_{m}{h}", (P, W), BF16) for h in range(NHC)]
             for m in "qkv"}
    gout = [[p.sbuf(f"gout{h}_{j}", (P, W), BF16) for j in range(2)] for h in range(NHC)]
    sqt = {m: [p.sbuf(f"sq_{m}{h}", (P, W), BF16) for h in range(NHC)] for m in "qk"}
    kstd = p.sbuf("kstd", (NHC, W), F32)
    krec = [p.sbuf(f"krec{j}", (NHC, W), F32) for j in range(2)]
    qstd = p.sbuf("qstd", (NHC, W), F32)
    qrec = [p.sbuf(f"qrec{j}", (NHC, W), F32) for j in range(2)]
    bsig = [p.sbuf(f"bsig{j}", (NHC, W), F32) for j in range(2)]
    khn = [[p.sbuf(f"khn{h}_{j}", (P, W), BF16) for j in range(2)] for h in range(NHC)]

    bank = [p.psum(f"bk{i}", (P, 512), F32) for i in range(8)]

    EPS6 = p.const(1e-6)
    EPSQ = p.const(128 * 1e-6)
    S128 = p.const(128.0)
    ini = p.chan("ini")
    wld = p.chan("wld", 16)
    xld = [p.chan("xld0", 16), p.chan("xld1", 16)]
    mm = p.chan("mm")
    ev = p.chan("ev")
    dv = p.chan("dv")
    gp = p.chan("gp")
    sod = p.chan("sod", 16)
    sog = p.chan("sog", 16)

    conv_eng = {}
    for idx, mh in enumerate([(m, h) for m in "qkv" for h in range(NHC)]):
        conv_eng[mh] = 0

    from collections import defaultdict
    tb = {
        "xld": [0] * NWIN,
        "mm_w": [[0] * 4 for _ in range(NWIN)],
        "mm_c1": [0] * NWIN,
        "mm_c2": [0] * NWIN,
        "ev_w": [[0] * 4 for _ in range(NWIN)],
        "ev_sil": [[defaultdict(int) for _ in range(4)] for _ in range(NWIN)],
        "ev_c1": [0] * NWIN,
        "cv_chain": [defaultdict(lambda: (0, 0)) for _ in range(NWIN)],
        "halo": [defaultdict(int) for _ in range(NWIN)],
        "dv_rec": [0] * NWIN,
        "dv_tt": [0] * NWIN,
        "end_ev": [0] * NWIN, "end_dv": [0] * NWIN, "end_gp": [0] * NWIN,
        "end_mm": [0] * NWIN, "end_sod": [0] * NWIN, "end_sog": [0] * NWIN,
    }

    WAVE_BANK = [0, 4, 0, 4]
    WMATS = "qkvg"

    def coarse_gate(eng, w):
        if w >= 2:
            p.wait(eng, ev, tb["end_ev"][w - 2])
            p.wait(eng, dv, tb["end_dv"][w - 2])
            p.wait(eng, gp, tb["end_gp"][w - 2])
            p.wait(eng, mm, tb["end_mm"][w - 2])
            p.wait(eng, sod, tb["end_sod"][w - 2])
            p.wait(eng, sog, tb["end_sog"][w - 2])

    def em_sync(s):
        for w in range(NWIN):
            coarse_gate(s, w)
            if w >= 2:
                p.wait(s, mm, tb["mm_c1"][w - 2])
            for d in range(DC):
                p.inc(s.dma_start(out=xnt[w % 2][d][:, :],
                                  in_=xn[d * P:(d + 1) * P, w * W:(w + 1) * W]),
                      xld[w % 2])
            tb["xld"][w] = p.count(xld[w % 2])
            j = w % 2
            for h in range(NHC):
                p.wait(s, ev, tb["ev_sil"][w][0][h])
                p.inc(s.dma_start(out=out["q"][h * P:(h + 1) * P, w * W:(w + 1) * W],
                                  in_=cout["q"][h][j][:, :]), sod)
            p.wait(s, dv, tb["dv_tt"][w])
            for h in range(NHC):
                p.inc(s.dma_start(out=out["k"][h * P:(h + 1) * P, w * W:(w + 1) * W],
                                  in_=khn[h][j][:, :]), sod)
            p.wait(s, dv, tb["dv_rec"][w])
            p.inc(s.dma_start(out=kapq_o[:, w * W:(w + 1) * W], in_=qrec[j][:, :]), sod)
            tb["end_sod"][w] = p.count(sod)

    def em_tensor(t):
        if not _is_dry(t):
            p.wait(t, ini, 1)
            p.wait(t, wld, 16 * (4 + 1 + 3 + 2))
        for w in range(NWIN):
            coarse_gate(t, w)
            p.wait(t, xld[w % 2], tb["xld"][w])
            for wv in range(4):
                m = WMATS[wv]
                bb = WAVE_BANK[wv]
                if wv >= 2:
                    p.wait(t, ev, tb["ev_w"][w][wv - 2])
                if w >= 1 and wv < 2:
                    p.wait(t, ev, tb["ev_c1"][w - 1])
                    p.wait(t, dv, tb["dv_tt"][w - 1])
                for h in range(NHC):
                    for d in range(DC):
                        i_ = t.matmul(bank[bb + h][:, :W],
                                      ws[m][:, d * HCOLS + h * P:d * HCOLS + (h + 1) * P],
                                      xnt[w % 2][d][:, :],
                                      start=(d == 0), stop=(d == DC - 1))
                    p.inc(i_, mm)
                tb["mm_w"][w][wv] = p.count(mm)
            p.wait(t, ev, tb["ev_w"][w][2])
            for d in range(DC):
                i_ = t.matmul(bank[0][0:NHC, :W],
                              wbs[:, d * NHC:(d + 1) * NHC],
                              xnt[w % 2][d][:, :],
                              start=(d == 0), stop=(d == DC - 1))
            p.inc(i_, mm)
            for mat, bk in (("q", 1), ("k", 2)):
                for h in range(NHC):
                    p.wait(t, ev, tb["ev_sil"][w][0 if mat == "q" else 1][h])
                    i_ = t.matmul(bank[bk][0:NHC, :W],
                                  ecols[:, h * NHC:(h + 1) * NHC],
                                  sqt[mat][h][:, :],
                                  start=(h == 0), stop=(h == NHC - 1))
                    p.inc(i_, mm)
            tb["mm_c1"][w] = p.count(mm)
            p.wait(t, ev, tb["ev_w"][w][3])
            p.wait(t, dv, tb["dv_rec"][w])
            for h in range(NHC):
                i_ = t.matmul(bank[3 + h][:, :W],
                              rsel[:, h * P:(h + 1) * P],
                              krec[w % 2][:, :W],
                              start=True, stop=True)
                p.inc(i_, mm)
            tb["mm_c2"][w] = p.count(mm)
            tb["end_mm"][w] = p.count(mm)

    def em_scalar(sc):
        if not _is_dry(sc):
            p.wait(sc, ini, 1)
        for w in range(NWIN):
            coarse_gate(sc, w)
            j = w % 2
            if w >= 1:
                for (mh, eng_ch) in conv_eng.items():
                    p.wait(sc, dv if eng_ch == 0 else gp, tb["halo"][w - 1][mh])
            for wv in range(4):
                m = WMATS[wv]
                bb = WAVE_BANK[wv]
                p.wait(sc, mm, tb["mm_w"][w][wv])
                for h in range(NHC):
                    if m == "g":
                        i_ = sc.copy(gout[h][j][:, :], bank[bb + h][:, :W])
                        p.inc(i_, ev)
                    else:
                        i_ = sc.copy(cin[m][h][:, 3:3 + W], bank[bb + h][:, :W])
                        p.inc(i_, ev)
                tb["ev_w"][w][wv] = p.count(ev)
            if w >= 1:
                p.wait(sc, mm, tb["mm_c1"][w - 1])
            sil = [defaultdict(int), defaultdict(int), defaultdict(int)]
            for m_i, m in enumerate("qkv"):
                for h in range(NHC):
                    eng_ch, cnt = tb["cv_chain"][w][(m, h)]
                    p.wait(sc, dv if eng_ch == 0 else gp, cnt)
                    i_ = sc.activation(cout[m][h][j][:, :], convf[m][h][:, :], AF.Silu)
                    if m in "qk":
                        sc.drain()
                        i_ = sc.activation(sqt[m][h][:, :], cout[m][h][j][:, :],
                                           AF.Square)
                    p.inc(i_, ev)
                    sil[m_i][h] = p.count(ev)
            tb["ev_sil"][w][0] = sil[0]
            tb["ev_sil"][w][1] = sil[1]
            tb["ev_sil"][w][2] = sil[2]
            p.wait(sc, mm, tb["mm_c1"][w])
            sc.activation(bsig[j][:, :], bank[0][0:NHC, :W], AF.Sigmoid)
            sc.drain()
            sc.activation(qstd[0:NHC, :], bank[1][0:NHC, :W], AF.Sqrt,
                          scale=S128[0:NHC, :], bias=EPSQ[0:NHC, :])
            sc.drain()
            i_ = sc.activation(kstd[0:NHC, :], bank[2][0:NHC, :W], AF.Sqrt,
                               bias=EPS6[0:NHC, :])
            p.inc(i_, ev)
            tb["ev_c1"][w] = p.count(ev)
            tb["end_ev"][w] = p.count(ev)

    def em_vector(v):
        if not _is_dry(v):
            p.emit_consts(v)
        for w in range(NWIN):
            coarse_gate(v, w)
            j = w % 2
            for (m, h), eng_ch in conv_eng.items():
                if eng_ch != 0:
                    continue
                wv = "qkv".index(m)
                p.wait(v, ev, tb["ev_w"][w][wv])
                ci = cin[m][h]
                cwcol = cws[m]
                if w == 0:
                    v.memset(ci[:, 0:3], 0.0)
                    v.drain()
                v.tensor_scalar(cs[0][0][:, :], ci[:, 0:W], cwcol[:, h * 4 + 0:h * 4 + 0 + 1], None,
                                op0=OP.mult)
                v.drain()
                v.scalar_tensor_tensor(cs[0][1][:, :], ci[:, 1:1 + W],
                                       cwcol[:, h * 4 + 1:h * 4 + 1 + 1], cs[0][0][:, :],
                                       op0=OP.mult, op1=OP.add)
                v.drain()
                v.scalar_tensor_tensor(cs[0][0][:, :], ci[:, 2:2 + W],
                                       cwcol[:, h * 4 + 2:h * 4 + 2 + 1], cs[0][1][:, :],
                                       op0=OP.mult, op1=OP.add)
                v.drain()
                if w >= 1:
                    p.wait(v, ev, tb["ev_sil"][w - 1]["qkv".index(m)][h])
                i_ = v.scalar_tensor_tensor(convf[m][h][:, :], ci[:, 3:3 + W],
                                            cwcol[:, h * 4 + 3:h * 4 + 3 + 1], cs[0][0][:, :],
                                            op0=OP.mult, op1=OP.add)
                p.inc(i_, dv)
                tb["cv_chain"][w][(m, h)] = (0, p.count(dv))
                v.drain()
                i_ = v.tensor_copy(ci[:, 0:3], ci[:, W:W + 3])
                p.inc(i_, dv)
                tb["halo"][w][(m, h)] = p.count(dv)
            p.wait(v, ev, tb["ev_c1"][w])
            v.reciprocal(qrec[j][:, :], qstd[0:NHC, :])
            v.drain()
            i_ = v.reciprocal(krec[j][:, :], kstd[0:NHC, :])
            p.inc(i_, dv)
            tb["dv_rec"][w] = p.count(dv)
            p.wait(v, mm, tb["mm_c2"][w])
            for h in range(NHC):
                i_ = v.tensor_tensor(khn[h][j][:, :], cout["k"][h][j][:, :],
                                     bank[3 + h][:, :W], op=OP.mult)
                p.inc(i_, dv)
                v.drain()
            tb["dv_tt"][w] = p.count(dv)
            tb["end_dv"][w] = p.count(dv)

    def em_gpsimd(g):
        for m in "qkvg":
            p.inc(g.dma_start(out=ws[m][:, :].rearrange("p (dc c) -> p dc c", c=HCOLS),
                              in_=wdr[m].rearrange("(dc p) c -> p dc c", p=P)), wld)
        p.inc(g.dma_start(out=wbs[:, :].rearrange("p (dc c) -> p dc c", c=NHC),
                          in_=wb.rearrange("(dc p) c -> p dc c", p=P),
                          allow_slow_non_contiguous=True), wld)
        for m in "qkv":
            p.inc(g.dma_start(out=cws[m][:, :].rearrange("p (hc k) -> p hc k", k=4),
                              in_=cw[m].rearrange("(hc p) k -> p hc k", p=P),
                              allow_slow_non_contiguous=True), wld)
        p.inc(g.dma_start(out=ecols[:, :], in_=ecols_d[:, :]), wld)
        p.inc(g.dma_start(out=rsel[:, :], in_=rowsel_d[:, :]), wld)
        for w in range(NWIN):
            coarse_gate(g, w)
            j = w % 2
            for (m, h), eng_ch in conv_eng.items():
                if eng_ch != 1:
                    continue
                wv = "qkv".index(m)
                p.wait(g, ev, tb["ev_w"][w][wv])
                ci = cin[m][h]
                cwcol = cws[m]
                if w == 0:
                    g.memset(ci[:, 0:3], 0.0)
                    g.drain()
                g.tensor_scalar(cs[1][0][:, :], ci[:, 0:W], cwcol[:, h * 4 + 0:h * 4 + 0 + 1], None,
                                op0=OP.mult)
                g.drain()
                g.scalar_tensor_tensor(cs[1][1][:, :], ci[:, 1:1 + W],
                                       cwcol[:, h * 4 + 1:h * 4 + 1 + 1], cs[1][0][:, :],
                                       op0=OP.mult, op1=OP.add)
                g.drain()
                g.scalar_tensor_tensor(cs[1][0][:, :], ci[:, 2:2 + W],
                                       cwcol[:, h * 4 + 2:h * 4 + 2 + 1], cs[1][1][:, :],
                                       op0=OP.mult, op1=OP.add)
                g.drain()
                if w >= 1:
                    p.wait(g, ev, tb["ev_sil"][w - 1]["qkv".index(m)][h])
                i_ = g.scalar_tensor_tensor(convf[m][h][:, :], ci[:, 3:3 + W],
                                            cwcol[:, h * 4 + 3:h * 4 + 3 + 1], cs[1][0][:, :],
                                            op0=OP.mult, op1=OP.add)
                p.inc(i_, gp)
                tb["cv_chain"][w][(m, h)] = (1, p.count(gp))
                g.drain()
                i_ = g.tensor_copy(ci[:, 0:3], ci[:, W:W + 3])
                p.inc(i_, gp)
                tb["halo"][w][(m, h)] = p.count(gp)
            tb["end_gp"][w] = p.count(gp)
            for h in range(NHC):
                p.wait(g, ev, tb["ev_sil"][w][2][h])
                p.inc(g.dma_start(out=out["v"][h * P:(h + 1) * P, w * W:(w + 1) * W],
                                  in_=cout["v"][h][j][:, :]), sog)
                p.wait(g, ev, tb["ev_w"][w][3])
                p.inc(g.dma_start(out=out["g"][h * P:(h + 1) * P, w * W:(w + 1) * W],
                                  in_=gout[h][j][:, :]), sog)
            p.wait(g, ev, tb["ev_c1"][w])
            p.inc(g.dma_start(out=beta_o[:, w * W:(w + 1) * W], in_=bsig[j][:, :]), sog)
            tb["end_sog"][w] = p.count(sog)
        p.wait(g, sog, tb["end_sog"][NWIN - 1])
        p.wait(g, sod, tb["end_sod"][NWIN - 1])

    for em in (em_sync, em_tensor, em_scalar, em_vector, em_gpsimd):
        em(DryEng())
    p.reset_counts()

    blk = p.block()
    if True:
        @blk.sync
        def _(s):
            em_sync(s)

        @blk.tensor
        def _(t):
            em_tensor(t)

        @blk.scalar
        def _(sc):
            em_scalar(sc)

        @blk.vector
        def _(v):
            em_vector(v)

        @blk.gpsimd
        def _(g):
            em_gpsimd(g)

    p.close()
    return p.nc



def build_l3(cfg):
    p = Prog()
    nc = p.nc
    T, NHC = cfg.T, cfg.NHC
    HCOLS = cfg.HCOLS
    NW = T // P
    GW = 4 if NW % 4 == 0 else 1
    NG = NW // GW

    kh = nc.dram_tensor("kh_d", [HCOLS, T], BF16, kind="ExternalInput").ap()
    qc = nc.dram_tensor("qc_d", [HCOLS, T], BF16, kind="ExternalInput").ap()
    gd = nc.dram_tensor("g_d", [HCOLS, T], BF16, kind="ExternalInput").ap()
    kt = nc.dram_tensor("kt", [T, HCOLS], BF16, kind="ExternalInput").ap()
    vt = nc.dram_tensor("vt", [T, HCOLS], BF16, kind="ExternalInput").ap()
    bt = nc.dram_tensor("beta_t", [T, NHC], F32, kind="ExternalInput").ap()
    kp = nc.dram_tensor("kap_t", [T, NHC], F32, kind="ExternalInput").ap()
    msl = nc.dram_tensor("strilneg", [P, P], F32, kind="ExternalInput").ap()
    mtu = nc.dram_tensor("triuincl", [P, P], F32, kind="ExternalInput").ap()
    idn = nc.dram_tensor("ident", [P, P], BF16, kind="ExternalInput").ap()
    nwr = nc.dram_tensor("normw_rep", [P, P], F32, kind="ExternalInput").ap()
    og = nc.dram_tensor("og_d", [HCOLS, T], BF16, kind="ExternalOutput").ap()

    kd_g = [p.sbuf(f"kd_g{j}", (P, GW * NHC * P), BF16) for j in range(2)]
    qd_g = [p.sbuf(f"qd_g{j}", (P, GW * NHC * P), BF16) for j in range(2)]
    gd_g = [p.sbuf(f"gd_g{j}", (P, GW * NHC * P), BF16) for j in range(2)]
    kt_g = [p.sbuf(f"kt_g{j}", (P, GW * NHC * P), BF16) for j in range(2)]
    vt_g = [p.sbuf(f"vt_g{j}", (P, GW * NHC * P), BF16) for j in range(2)]
    bk_g = [p.sbuf(f"bk_g{j}", (P, GW * NHC), F32) for j in range(2)]
    kp_g = [p.sbuf(f"kp_g{j}", (P, GW * NHC), F32) for j in range(2)]
    sln = p.sbuf("sln", (P, P), F32)
    tui = p.sbuf("tui", (P, P), F32)
    idb = p.sbuf("idb", (P, P), BF16)
    nwrs = p.sbuf("nwrs", (P, P), F32)
    rhs0 = p.sbuf("rhs0", (P, 2 * P), BF16)
    xa = p.sbuf("xa", (P, 2 * P), BF16)
    xb = p.sbuf("xb", (P, 2 * P), BF16)
    nmat = p.sbuf("nmat", (P, P), BF16)
    ntr = p.sbuf("ntr", (P, P), BF16)
    ntp = [p.sbuf(f"ntp{j}", (P, P), BF16) for j in range(6)]
    nup = [p.sbuf(f"nup{j}", (P, P), BF16) for j in range(5)]
    mt = p.sbuf("mt", (P, P), BF16)
    wtr = p.sbuf("wtr", (P, P), BF16)
    sbf = p.sbuf("sbf", (P, P), BF16)
    stbf = p.sbuf("stbf", (P, P), BF16)
    ub = p.sbuf("ub", (P, P), BF16)
    o1 = p.sbuf("o1", (P, P), BF16)
    osq = p.sbuf("osq", (P, P), BF16)
    so2 = p.sbuf("so2", (P, 1), F32)
    rms_s = p.sbuf("rms_s", (P, 1), F32)
    rms = p.sbuf("rms", (P, 1), F32)
    csc = p.sbuf("csc", (P, 1), F32)
    sig = p.sbuf("sig", (P, P), BF16)
    ogst = [p.sbuf(f"ogst{j}", (P, NHC * P), BF16) for j in range(2)]

    bS = p.psum("bS", (P, 512), F32)
    bSt = p.psum("bSt", (P, 512), F32)
    bG = p.psum("bG", (P, 512), F32)
    bP1 = p.psum("bP1", (P, 512), F32)
    bP2 = p.psum("bP2", (P, 512), F32)
    bTR = p.psum("bTR", (P, 512), BF16)
    bO = p.psum("bO", (P, 512), F32)
    bU = p.psum("bU", (P, 512), F32)

    S128i = p.const(1.0 / 128.0)
    EPS5 = p.const(1e-5)
    ini = p.chan("ini")
    ldm = p.chan("ldm", 16)
    ldh = [p.chan("l3h0", 16), p.chan("l3h1", 16)]
    ldg = [p.chan("l3g0", 16), p.chan("l3g1", 16)]
    mm = p.chan("mm")
    dvv = p.chan("dvv")
    acc = p.chan("acc")
    ost = [p.chan("ost0", 16), p.chan("ost1", 16)]

    def schedule(E):
        t, v, sc, s, g = E["t"], E["v"], E["sc"], E["s"], E["g"]
        p.wait(t, ini, 1)
        p.wait(v, ini, 1)
        p.wait(sc, ini, 1)
        for dst, srcap in ((sln, msl), (tui, mtu), (idb, idn), (nwrs, nwr)):
            p.inc(s.dma_start(out=dst[:, :], in_=srcap[:, :]), ldm)
        mask_ld = p.count(ldm)

        grp_end = {}
        ld_done = {}

        for gi in range(NG):
            j = gi % 2
            if gi >= 2:
                e = grp_end[gi - 2]
                for cn in ("mm", "dvv", "acc"):
                    p.wait(s, p.chan(cn), e[cn])
                    p.wait(g, p.chan(cn), e[cn])
            t0 = gi * GW * P
            gcols = GW * NHC * P
            for wi_ in range(GW):
                p.inc(s.dma_start(
                    out=kd_g[j][:, wi_ * NHC * P:(wi_ + 1) * NHC * P]
                    .rearrange("p (hc t) -> p hc t", t=P),
                    in_=kh[:, t0 + wi_ * P:t0 + (wi_ + 1) * P]
                    .rearrange("(hc p) t -> p hc t", p=P)), ldh[gi % 2])
            for wi_ in range(GW):
                p.inc(s.dma_start(
                    out=qd_g[j][:, wi_ * NHC * P:(wi_ + 1) * NHC * P]
                    .rearrange("p (hc t) -> p hc t", t=P),
                    in_=qc[:, t0 + wi_ * P:t0 + (wi_ + 1) * P]
                    .rearrange("(hc p) t -> p hc t", p=P)), ldh[gi % 2])
            for wi_ in range(GW):
                p.inc(s.dma_start(
                    out=gd_g[j][:, wi_ * NHC * P:(wi_ + 1) * NHC * P]
                    .rearrange("p (hc t) -> p hc t", t=P),
                    in_=gd[:, t0 + wi_ * P:t0 + (wi_ + 1) * P]
                    .rearrange("(hc p) t -> p hc t", p=P)), ldh[gi % 2])
            p.inc(g.dma_start(
                out=kt_g[j][:, :].rearrange("p (w c) -> p w c", c=NHC * P),
                in_=kt[t0:t0 + GW * P, :].rearrange("(w p) c -> p w c", p=P)),
                ldg[gi % 2])
            p.inc(g.dma_start(
                out=vt_g[j][:, :].rearrange("p (w c) -> p w c", c=NHC * P),
                in_=vt[t0:t0 + GW * P, :].rearrange("(w p) c -> p w c", p=P)),
                ldg[gi % 2])
            p.inc(g.dma_start(
                out=bk_g[j][:, :].rearrange("p (w c) -> p w c", c=NHC),
                in_=bt[t0:t0 + GW * P, :].rearrange("(w p) c -> p w c", p=P),
                allow_slow_non_contiguous=True), ldg[gi % 2])
            p.inc(g.dma_start(
                out=kp_g[j][:, :].rearrange("p (w c) -> p w c", c=NHC),
                in_=kp[t0:t0 + GW * P, :].rearrange("(w p) c -> p w c", p=P),
                allow_slow_non_contiguous=True), ldg[gi % 2])
            ld_done[gi] = (p.count(ldh[gi % 2]), p.count(ldg[gi % 2]))

            for wi in range(GW):
                w = gi * GW + wi
                oj = w % 2
                for h in range(NHC):
                    kd_w = kd_g[j][:, (wi * NHC + h) * P:(wi * NHC + h + 1) * P]
                    qd_w = qd_g[j][:, (wi * NHC + h) * P:(wi * NHC + h + 1) * P]
                    gd_w = gd_g[j][:, (wi * NHC + h) * P:(wi * NHC + h + 1) * P]
                    kt_w = kt_g[j][:, wi * NHC * P + h * P: wi * NHC * P + (h + 1) * P]
                    vt_w = vt_g[j][:, wi * NHC * P + h * P: wi * NHC * P + (h + 1) * P]
                    bcol = bk_g[j][:, wi * NHC + h: wi * NHC + h + 1]
                    kcol = kp_g[j][:, wi * NHC + h: wi * NHC + h + 1]

                    p.wait(v, ldg[gi % 2], ld_done[gi][1])
                    p.wait(v, mm, p.count(mm))
                    v.tensor_scalar(rhs0[:, 0:P], kt_w, bcol, None, op0=OP.mult)
                    v.drain()
                    i_ = p.inc(v.tensor_scalar(rhs0[:, P:2 * P], vt_w, bcol, None,
                                               op0=OP.mult), dvv)
                    rhs0_dv = p.count(dvv)
                    p.wait(t, ldh[gi % 2], ld_done[gi][0])
                    p.wait(t, dvv, p.count(dvv))
                    t.matmul(bG[:, 0:P], kd_w, kd_w, start=True, stop=True)
                    i_ = t.matmul(bG[:, P:2 * P], kd_w, qd_w, start=True, stop=True)
                    p.inc(i_, mm)
                    gram_mm = p.count(mm)
                    p.wait(v, mm, gram_mm)
                    p.wait(v, ldm, mask_ld)
                    v.scalar_tensor_tensor(nmat[:, :], bG[:, 0:P], bcol, sln[:, :],
                                           op0=OP.mult, op1=OP.mult)
                    v.drain()
                    i_ = p.inc(v.tensor_tensor(mt[:, :], bG[:, P:2 * P], tui[:, :],
                                               op=OP.mult), dvv)
                    mask_dv = p.count(dvv)
                    p.wait(t, dvv, mask_dv)
                    i_ = p.inc(t.transpose(bTR[:, 0:P], nmat[:, :], idb[:, :]), mm)
                    p.wait(sc, mm, p.count(mm))
                    i_ = p.inc(sc.copy(ntr[:, :], bTR[:, 0:P]), acc)
                    ntr_ac = p.count(acc)
                    a_cur, b_cur = nmat[:, :], ntr[:, :]
                    for lv in range(6):
                        bank = (bP1, bP2)[lv % 2]
                        p.wait(t, acc if lv >= 2 else acc,
                               p.count(acc))
                        if lv == 0:
                            p.wait(t, acc, ntr_ac)
                        i_ = p.inc(t.matmul(bank[:, 0:P], a_cur, b_cur,
                                            start=True, stop=True), mm)
                        p.wait(sc, mm, p.count(mm))
                        i_ = p.inc(sc.copy(ntp[lv][:, :], bank[:, 0:P]), acc)
                        if lv < 5:
                            i_ = p.inc(t.matmul(bank[:, P:2 * P], b_cur, a_cur,
                                                start=True, stop=True), mm)
                            p.wait(sc, mm, p.count(mm))
                            i_ = p.inc(sc.copy(nup[lv][:, :], bank[:, P:2 * P]), acc)
                            a_cur, b_cur = nup[lv][:, :], ntp[lv][:, :]
                    xs = [rhs0, xa, xb, xa, xb, xa, xb, xa]
                    tfs = [ntr[:, :]] + [ntp[lv][:, :] for lv in range(6)]
                    ap_dv = 0
                    for jj in range(7):
                        bank = (bP2, bP1)[jj % 2]
                        p.wait(t, dvv, p.count(dvv))
                        p.wait(t, acc, p.count(acc))
                        if jj == 0:
                            p.wait(t, dvv, rhs0_dv)
                        i_ = p.inc(t.matmul(bank[:, 0:2 * P], tfs[jj], xs[jj][:, :],
                                            start=True, stop=True), mm)
                        p.wait(v, mm, p.count(mm))
                        v.drain()
                        i_ = p.inc(v.tensor_tensor(xs[jj + 1][:, :], xs[jj][:, :],
                                                   bank[:, 0:2 * P], op=OP.add), dvv)
                    xfin = xs[7]
                    ap_dv = p.count(dvv)
                    p.wait(t, dvv, ap_dv)
                    i_ = p.inc(t.transpose(bTR[:, P:2 * P], xfin[:, 0:P], idb[:, :]), mm)
                    p.wait(sc, mm, p.count(mm))
                    i_ = p.inc(sc.copy(wtr[:, :], bTR[:, P:2 * P]), acc)
                    wtr_ac = p.count(acc)
                    p.wait(v, mm, p.count(mm))
                    if w == 0:
                        v.memset(sbf[:, :], 0.0)
                        i_ = p.inc(v.memset(stbf[:, :], 0.0), dvv)
                    else:
                        v.tensor_copy(sbf[:, :], bS[:, h * P:(h + 1) * P])
                        v.drain()
                        i_ = p.inc(v.tensor_copy(stbf[:, :], bSt[:, h * P:(h + 1) * P]),
                                   dvv)
                    scast_dv = p.count(dvv)
                    p.wait(t, acc, wtr_ac)
                    p.wait(t, dvv, scast_dv)
                    i_ = p.inc(t.matmul(bU[:, 0:P], wtr[:, :], sbf[:, :],
                                        start=True, stop=True), mm)
                    p.wait(v, mm, p.count(mm))
                    i_ = p.inc(v.tensor_tensor(ub[:, :], xfin[:, P:2 * P], bU[:, 0:P],
                                               op=OP.subtract), dvv)
                    ub_dv = p.count(dvv)
                    p.wait(t, dvv, ub_dv)
                    t.matmul(bO[:, 0:P], qd_w, sbf[:, :], start=True, stop=False)
                    i_ = p.inc(t.matmul(bO[:, 0:P], mt[:, :], ub[:, :],
                                        start=False, stop=True), mm)
                    o_mm = p.count(mm)
                    t.matmul(bS[:, h * P:(h + 1) * P], kt_w, ub[:, :],
                             start=(w == 0), stop=(w == NW - 1), skip_group_check=True)
                    i_ = p.inc(t.matmul(bSt[:, h * P:(h + 1) * P], ub[:, :], kt_w,
                                        start=(w == 0), stop=(w == NW - 1),
                                        skip_group_check=True), mm)
                    p.wait(sc, mm, o_mm)
                    sc.activation(osq[:, :], bO[:, 0:P], AF.Square, scale=kcol,
                                  accum_out=so2[:, :])
                    sc.drain()
                    i_ = p.inc(sc.activation(rms_s[:, :], so2[:, :], AF.Sqrt,
                                             scale=S128i[:, :], bias=EPS5[:, :]), acc)
                    rms_ac = p.count(acc)
                    p.wait(v, acc, rms_ac)
                    v.reciprocal(rms[:, :], rms_s[:, :])
                    v.drain()
                    v.tensor_tensor(csc[:, :], rms[:, :], kcol, op=OP.mult)
                    v.drain()
                    i_ = p.inc(v.scalar_tensor_tensor(o1[:, :], bO[:, 0:P], csc[:, :],
                                                      nwrs[:, :], op0=OP.mult,
                                                      op1=OP.mult), dvv)
                    o1_dv = p.count(dvv)
                    p.wait(t, dvv, o1_dv)
                    i_ = p.inc(t.transpose(bTR[:, 2 * P:3 * P], o1[:, :], idb[:, :]), mm)
                    ot_mm = p.count(mm)
                    p.wait(sc, ldh[gi % 2], ld_done[gi][0])
                    i_ = p.inc(sc.activation(sig[:, :], gd_w, AF.Sigmoid), acc)
                    sig_ac = p.count(acc)
                    p.wait(v, mm, ot_mm)
                    p.wait(v, acc, sig_ac)
                    if w >= 2 and h == 0:
                        p.wait(v, ost[w % 2], ogst_st[w - 2])
                    i_ = p.inc(v.tensor_tensor(ogst[oj][:, h * P:(h + 1) * P],
                                               bTR[:, 2 * P:3 * P], sig[:, :],
                                               op=OP.mult), dvv)
                og_dv = p.count(dvv)
                p.wait(s, dvv, og_dv)
                p.inc(s.dma_start(
                    out=og[:, w * P:(w + 1) * P]
                    .rearrange("(hc p) t -> p hc t", p=P),
                    in_=ogst[oj][:, :].rearrange("p (hc t) -> p hc t", t=P)),
                    ost[w % 2])
                ogst_st[w] = p.count(ost[w % 2])
            grp_end[gi] = {"mm": p.count(mm), "dvv": p.count(dvv),
                           "acc": p.count(acc)}
        p.wait(s, ost[0], p.count("ost0"))
        p.wait(s, ost[1], p.count("ost1"))

    ogst_st = {}

    engines = ["t", "v", "sc", "s", "g"]
    blk = p.block()
    if True:
        @blk.tensor
        def _(te):
            p.reset_counts()
            ogst_st.clear()
            schedule({"t": te, "v": DryEng(), "sc": DryEng(), "s": DryEng(),
                      "g": DryEng()})

        @blk.vector
        def _(ve):
            p.reset_counts()
            ogst_st.clear()
            p.emit_consts(ve)
            schedule({"t": DryEng(), "v": ve, "sc": DryEng(), "s": DryEng(),
                      "g": DryEng()})

        @blk.scalar
        def _(sce):
            p.reset_counts()
            ogst_st.clear()
            schedule({"t": DryEng(), "v": DryEng(), "sc": sce, "s": DryEng(),
                      "g": DryEng()})

        @blk.sync
        def _(se):
            p.reset_counts()
            ogst_st.clear()
            schedule({"t": DryEng(), "v": DryEng(), "sc": DryEng(), "s": se,
                      "g": DryEng()})

        @blk.gpsimd
        def _(ge):
            p.reset_counts()
            ogst_st.clear()
            schedule({"t": DryEng(), "v": DryEng(), "sc": DryEng(),
                      "s": DryEng(), "g": ge})

    p.close()
    return p.nc



def build_l4(cfg):
    p = Prog()
    nc = p.nc
    D, TQ, DC, NE, IC = cfg.D, cfg.TQ, cfg.DC, cfg.NE, cfg.IC
    INTER = cfg.INTER
    NTW = TQ // 512
    NE_T = NE + cfg.NZ

    xnd = nc.dram_tensor("xn_fm", [D, TQ], BF16, kind="ExternalInput").ap()
    ofd = nc.dram_tensor("o_fm", [D, TQ], BF16, kind="ExternalInput").ap()
    xfd = nc.dram_tensor("x_fm", [D, TQ], F32, kind="ExternalInput").ap()
    wcd = nc.dram_tensor("wcls", [D, NE_T], F32, kind="ExternalInput").ap()
    ebd = nc.dram_tensor("ebias_rep", [P, NE_T], F32, kind="ExternalInput").ap()
    rsd = nc.dram_tensor("rsel12", [16, 16 * P], F32, kind="ExternalInput").ap()
    idf = nc.dram_tensor("identf", [P, P], F32, kind="ExternalInput").ap()
    gtd = nc.dram_tensor("gateT", [NE, D, INTER], F32, kind="ExternalInput").ap()
    upd = nc.dram_tensor("upT", [NE, D, INTER], F32, kind="ExternalInput").ap()
    dnd = nc.dram_tensor("downT", [NE, INTER, D], F32, kind="ExternalInput").ap()
    wod = nc.dram_tensor("wo", [D, D], F32, kind="ExternalInput").ap()
    outd = nc.dram_tensor("out_fm", [D, TQ], F32, kind="ExternalOutput").ap()

    dn_bf = nc.dram_tensor("dn_bf", [NE * INTER, D], BF16).ap()
    wo_bf = nc.dram_tensor("wo_bf", [D, D], BF16).ap()
    act_sp = nc.dram_tensor("act_sp", [NE * INTER, TQ], BF16).ap()

    wcls_s = p.sbuf("wcls_s", (P, DC * NE_T), BF16)
    ebias_s = p.sbuf("ebias_s", (P, NE_T), F32)
    rsel_s = p.sbuf("rsel_s", (16, 16 * P), F32)
    idf_s = p.sbuf("idf_s", (P, P), F32)
    route_d = p.sbuf("route_d", (16, TQ), F32)
    xnw = [[p.sbuf(f"xnw{b}_{d}", (P, 512), BF16) for d in range(DC)] for b in range(2)]
    SLAB = max(2 * DC * INTER, NE * IC * 512)
    slab1 = p.sbuf("slab1", (P, SLAB), BF16)
    dn_s = [p.sbuf(f"dn_s{b}", (P, NE * IC * P), BF16) for b in range(2)]
    wo_s = [p.sbuf(f"wo_s{b}", (P, DC * P), BF16) for b in range(2)]
    ofm = [p.sbuf(f"ofm{d}", (P, 512), BF16) for d in range(DC)]
    xfm = [p.sbuf(f"xfm{b}", (P, 512), F32) for b in range(2)]
    outt = [p.sbuf(f"outt{b}", (P, 512), F32) for b in range(2)]
    wrep = p.sbuf("wrep", (P, 512), BF16)
    zrep = p.sbuf("zrep", (P, 512), BF16)
    sgt = p.sbuf("sgt", (P, 512), BF16)
    t1b = p.sbuf("t1b", (P, 512), BF16)
    actt = [p.sbuf(f"actt{b}", (P, 512), BF16) for b in range(2)]
    t2f = p.sbuf("t2f", (P, 512), F32)
    glf = p.sbuf("glf", (P, 512), F32)
    exf = p.sbuf("exf", (P, NE_T), F32)
    mrow = p.sbuf("mrow", (P, 1), F32)
    nmrow = p.sbuf("nmrow", (P, 1), F32)
    srow = p.sbuf("srow", (P, 1), F32)
    rsrow = p.sbuf("rsrow", (P, 1), F32)
    biased = p.sbuf("biased", (P, NE_T), F32)
    tq1 = p.sbuf("tq1", (P, NE_T), F32)
    tq2 = p.sbuf("tq2", (P, NE_T), F32)
    maskge = p.sbuf("maskge", (P, NE_T), F32)
    p15 = p.sbuf("p15", (P, NE_T), F32)
    routp = p.sbuf("routp", (P, 16), F32)
    zs1 = p.sbuf("zs1", (P, 1), F32)

    bank = [p.psum(f"l4b{i}", (P, 512), F32) for i in range(8)]

    ini = p.chan("ini")
    cst = p.chan("cst", 16)
    cstw = p.chan("cstw", 16)
    cstg = [p.chan("cstg0", 16), p.chan("cstg1", 16)]
    ldx = [p.chan("ldx0", 16), p.chan("ldx1", 16)]
    ldw = [p.chan("ldw0", 16), p.chan("ldw1", 16)]
    lda = [p.chan("lda0", 16), p.chan("lda1", 16)]
    spl = [p.chan("spl0", 16), p.chan("spl1", 16)]
    ldo = p.chan("ldo", 16)
    ost = [p.chan("osA", 16), p.chan("osB", 16)]
    mm = p.chan("mm")
    dvv = p.chan("dvv")
    acc = p.chan("acc")

    NSUB = 4

    def schedule(E):
        t, v, sc, s, g = E["t"], E["v"], E["sc"], E["s"], E["g"]
        for e_ in (t, v, sc):
            p.wait(e_, ini, 1)
        p.inc(g.dma_start(out=wcls_s[:, :].rearrange("p (dc c) -> p dc c", c=NE_T),
                          in_=wcd.rearrange("(dc p) c -> p dc c", p=P),
                          allow_slow_non_contiguous=True), cstw)
        for e in range(NE):
            p.inc(g.dma_start(out=dn_bf[e * INTER:(e + 1) * INTER, :],
                              in_=dnd[e, :, :]), cst)
        p.inc(g.dma_start(out=wo_bf[:, :], in_=wod[:, :]), cst)
        p.inc(s.dma_start(out=ebias_s[:, :], in_=ebd[:, :]), ldo)
        p.inc(s.dma_start(out=rsel_s[:, :], in_=rsd[:, :]), ldo)
        p.inc(s.dma_start(out=idf_s[:, :], in_=idf[:, :]), ldo)
        small_ld = p.count(ldo)
        cast_done = p.count(cst)

        rt_done = [0] * NTW
        for w in range(NTW):
            j = w % 2
            if w >= 2:
                p.wait(s, mm, rt_done[w - 2])
            for d in range(DC):
                p.inc(s.dma_start(out=xnw[j][d][:, :],
                                  in_=xnd[d * P:(d + 1) * P, w * 512:(w + 1) * 512]),
                      ldx[j])
            xld = p.count(ldx[j])
            for sb in range(NSUB):
                p.wait(t, ldx[j], xld)
                p.wait(t, cstw, 16)
                if w > 0 or sb > 0:
                    p.wait(t, dvv, p.count(dvv))
                    p.wait(t, acc, p.count(acc))
                for d in range(DC):
                    i_ = t.matmul(bank[0][0:P, 0:NE_T],
                                  xnw[j][d][:, sb * P:(sb + 1) * P],
                                  wcls_s[:, d * NE_T:(d + 1) * NE_T],
                                  start=(d == 0), stop=(d == DC - 1))
                p.inc(i_, mm)
                cls_mm = p.count(mm)
                p.wait(v, mm, cls_mm)
                p.wait(v, ldo, small_ld)
                v.reduce_max(mrow[:, :], bank[0][0:P, 0:NE_T], axis=AX.X)
                v.drain()
                i_ = p.inc(v.tensor_scalar_mul(nmrow[:, :], mrow[:, :], -1.0), dvv)
                nm_dv = p.count(dvv)
                p.wait(sc, dvv, nm_dv)
                i_ = p.inc(sc.activation(exf[:, :], bank[0][0:P, 0:NE_T], AF.Exp,
                                         bias=nmrow[:, :], accum_out=srow[:, :]), acc)
                ex_ac = p.count(acc)
                p.wait(v, acc, ex_ac)
                v.reciprocal(rsrow[:, :], srow[:, :])
                v.drain()
                v.scalar_tensor_tensor(biased[:, :], exf[:, :], rsrow[:, :],
                                       ebias_s[:, :], op0=OP.mult, op1=OP.add)
                v.drain()
                v.reduce_max(tq1[:, 0:1], biased[:, :], axis=AX.X)
                v.drain()
                v.tensor_scalar(tq2[:, :], biased[:, :], tq1[:, 0:1], None,
                                op0=OP.is_equal)
                v.drain()
                v.scalar_tensor_tensor(maskge[:, :], tq2[:, :], -1e9, biased[:, :],
                                       op0=OP.mult, op1=OP.add)
                v.drain()
                v.reduce_max(tq1[:, 1:2], maskge[:, :], axis=AX.X)
                v.drain()
                v.tensor_scalar(maskge[:, :], biased[:, :], tq1[:, 1:2], None,
                                op0=OP.is_ge)
                v.drain()
                v.tensor_scalar(p15[:, :], exf[:, :], rsrow[:, :], 1.5,
                                op0=OP.mult, op1=OP.mult)
                v.drain()
                v.tensor_tensor(routp[:, 0:NE_T], p15[:, :], maskge[:, :],
                                op=OP.mult)
                v.drain()
                v.reduce_sum(routp[:, NE_T:NE_T + 1], routp[:, NE:NE_T], axis=AX.X)
                v.drain()
                i_ = p.inc(v.memset(routp[:, NE_T + 1:16], 0.0), dvv)
                rp_dv = p.count(dvv)
                p.wait(t, dvv, rp_dv)
                p.wait(t, ldo, small_ld)
                i_ = p.inc(t.transpose(bank[1][0:16, 0:P], routp[:, 0:16],
                                       idf_s[:, :]), mm)
                tr_mm = p.count(mm)
                p.wait(sc, mm, tr_mm)
                i_ = p.inc(sc.copy(route_d[0:16, w * 512 + sb * P:
                                           w * 512 + (sb + 1) * P],
                                   bank[1][0:16, 0:P]), acc)
            rt_done[w] = p.count(mm)
        route_ac = p.count(acc)

        ga_end = {}
        for e in range(NE):
            if e >= 1:
                p.wait(g, mm, ga_end[e - 1])
            p.inc(g.dma_start(
                out=slab1[:, 0:DC * INTER].rearrange("p (dc i) -> p dc i", i=INTER),
                in_=gtd[e, :, :].rearrange("(dc p) i -> p dc i", p=P)), cstg[e % 2])
            p.inc(g.dma_start(
                out=slab1[:, DC * INTER:2 * DC * INTER]
                .rearrange("p (dc i) -> p dc i", i=INTER),
                in_=upd[e, :, :].rearrange("(dc p) i -> p dc i", p=P)), cstg[e % 2])
            gu_ld = p.count(cstg[e % 2])
            for w in range(NTW):
                j = w % 2
                if True:
                    p.wait(s, mm, p.count(mm))
                    for d in range(DC):
                        p.inc(s.dma_start(
                            out=xnw[j][d][:, :],
                            in_=xnd[d * P:(d + 1) * P, w * 512:(w + 1) * 512]),
                            ldx[j])
                xld = p.count(ldx[j])
                p.wait(t, acc, route_ac)
                p.wait(t, dvv, p.count(dvv))
                p.wait(t, acc, p.count(acc))
                i_ = p.inc(t.matmul(bank[6][:, :], rsel_s[:, e * P:(e + 1) * P],
                                    route_d[0:16, w * 512:(w + 1) * 512],
                                    start=True, stop=True), mm)
                p.wait(sc, mm, p.count(mm))
                p.wait(sc, dvv, p.count(dvv))
                i_ = p.inc(sc.copy(wrep[:, :], bank[6][:, :]), acc)
                wrep_ac = p.count(acc)
                for i in range(IC):
                    bg = bank[i % 2]
                    bu = bank[2 + i % 2]
                    p.wait(t, ldx[j], xld)
                    p.wait(t, cstg[e % 2], gu_ld)
                    p.wait(t, acc, p.count(acc))
                    p.wait(t, dvv, p.count(dvv))
                    for d in range(DC):
                        t.matmul(bg[:, :], slab1[:, d * INTER + i * P:
                                                 d * INTER + (i + 1) * P],
                                 xnw[j][d][:, :], start=(d == 0), stop=(d == DC - 1))
                    for d in range(DC):
                        i_ = t.matmul(bu[:, :],
                                      slab1[:, DC * INTER + d * INTER + i * P:
                                            DC * INTER + d * INTER + (i + 1) * P],
                                      xnw[j][d][:, :], start=(d == 0),
                                      stop=(d == DC - 1))
                    p.inc(i_, mm)
                    gu_mm = p.count(mm)
                    p.wait(sc, mm, gu_mm)
                    p.wait(sc, dvv, p.count(dvv))
                    i_ = p.inc(sc.activation(sgt[:, :], bg[:, :], AF.Silu), acc)
                    sg_ac = p.count(acc)
                    p.wait(v, acc, sg_ac)
                    p.wait(v, acc, wrep_ac)
                    if True:
                        p.wait(v, spl[(w * IC + i) % 2], p.count(
                            p.chan(f"spl{(w * IC + i) % 2}")))
                    v.tensor_tensor(t1b[:, :], sgt[:, :], bu[:, :], op=OP.mult)
                    v.drain()
                    i_ = p.inc(v.tensor_tensor(actt[(w * IC + i) % 2][:, :],
                                               t1b[:, :], wrep[:, :], op=OP.mult),
                               dvv)
                    act_dv = p.count(dvv)
                    p.wait(g, dvv, act_dv)
                    p.inc(g.dma_start(
                        out=act_sp[e * INTER + i * P:e * INTER + (i + 1) * P,
                                   w * 512:(w + 1) * 512],
                        in_=actt[(w * IC + i) % 2][:, :]), spl[(w * IC + i) % 2])
            ga_end[e] = p.count(mm)
        spill_all = (p.count("spl0"), p.count("spl1"))

        fin_done = [0] * NTW
        for w in range(NTW):
            j = w % 2
            if w == 0:
                p.wait(s, spl[0], spill_all[0])
                p.wait(s, spl[1], spill_all[1])
                p.wait(s, mm, ga_end[NE - 1])
                p.wait(s, cst, cast_done)
            if w >= 1:
                p.wait(s, mm, fin_done[w - 1])
            p.inc(s.dma_start(
                out=slab1[:, 0:NE * IC * 512].rearrange("p (ei t2) -> p ei t2",
                                                        t2=512),
                in_=act_sp[:, w * 512:(w + 1) * 512]
                .rearrange("(ei p) t2 -> p ei t2", p=P)), lda[j])
            for d in range(DC):
                p.inc(s.dma_start(out=xnw[j][d][:, :],
                                  in_=xnd[d * P:(d + 1) * P, w * 512:(w + 1) * 512]),
                      ldx[j])
                p.inc(s.dma_start(out=ofm[d][:, :],
                                  in_=ofd[d * P:(d + 1) * P, w * 512:(w + 1) * 512]),
                      ldx[j])
            ldb = (p.count(lda[j]), p.count(ldx[j]))
            p.wait(t, ldo, small_ld)
            p.wait(t, dvv, p.count(dvv))
            p.wait(t, acc, p.count(acc))
            i_ = p.inc(t.matmul(bank[6][:, :], rsel_s[:, 12 * P:13 * P],
                                route_d[0:16, w * 512:(w + 1) * 512],
                                start=True, stop=True), mm)
            p.wait(sc, mm, p.count(mm))
            i_ = p.inc(sc.copy(zrep[:, :], bank[6][:, :]), acc)
            zrep_ac = p.count(acc)
            for oc in range(DC):
                jo = oc % 2
                p.wait(s, mm, p.count(mm))
                p.inc(s.dma_start(
                    out=dn_s[jo][:, :].rearrange("p (ei c) -> p ei c", c=P),
                    in_=dn_bf[:, oc * P:(oc + 1) * P]
                    .rearrange("(ei p) c -> p ei c", p=P)), ldw[jo])
                p.inc(s.dma_start(
                    out=wo_s[jo][:, :].rearrange("p (dc c) -> p dc c", c=P),
                    in_=wo_bf[:, oc * P:(oc + 1) * P]
                    .rearrange("(dc p) c -> p dc c", p=P)), ldw[jo])
                dn_ld = p.count(ldw[jo])
                bf_ = bank[4 + oc % 2]
                bs_ = bank[oc % 2]
                p.wait(t, ldw[jo], dn_ld)
                p.wait(t, lda[j], ldb[0])
                p.wait(t, ldx[j], ldb[1])
                p.wait(t, acc, p.count(acc))
                p.wait(t, dvv, p.count(dvv))
                for ei in range(NE * IC):
                    i_ = t.matmul(bf_[:, :], dn_s[jo][:, ei * P:(ei + 1) * P],
                                  slab1[:, ei * 512:(ei + 1) * 512],
                                  start=(ei == 0), stop=(ei == NE * IC - 1))
                p.inc(i_, mm)
                for d in range(DC):
                    i_ = t.matmul(bs_[:, :], wo_s[jo][:, d * P:(d + 1) * P],
                                  ofm[d][:, :], start=(d == 0), stop=(d == DC - 1))
                p.inc(i_, mm)
                dw_mm = p.count(mm)
                p.wait(v, mm, dw_mm)
                p.wait(v, acc, zrep_ac)
                v.tensor_tensor(t1b[:, :], xnw[j][oc][:, :], zrep[:, :], op=OP.mult)
                v.drain()
                i_ = p.inc(v.tensor_tensor(t2f[:, :], t1b[:, :], bf_[:, :],
                                           op=OP.add), dvv)
                t2_dv = p.count(dvv)
                p.wait(sc, dvv, t2_dv)
                i_ = p.inc(sc.activation(glf[:, :], t2f[:, :], AF.Silu), acc)
                gl_ac = p.count(acc)
                p.wait(s, dvv, p.count(dvv))
                p.inc(s.dma_start(out=xfm[j][:, :],
                                  in_=xfd[oc * P:(oc + 1) * P,
                                          w * 512:(w + 1) * 512]), ldx[j])
                xf_ld = p.count(ldx[j])
                p.wait(v, acc, gl_ac)
                p.wait(v, ldx[j], xf_ld)
                p.wait(v, ost[j], p.count("osA" if j == 0 else "osB"))
                v.tensor_tensor(t2f[:, :], bs_[:, :], glf[:, :], op=OP.mult)
                v.drain()
                i_ = p.inc(v.tensor_tensor(outt[j][:, :], t2f[:, :], xfm[j][:, :],
                                           op=OP.add), dvv)
                out_dv = p.count(dvv)
                p.wait(g, dvv, out_dv)
                p.inc(g.dma_start(out=outd[oc * P:(oc + 1) * P,
                                           w * 512:(w + 1) * 512],
                                  in_=outt[j][:, :]), ost[j])
            fin_done[w] = p.count(mm)
        p.wait(g, ost[0], p.count("osA"))
        p.wait(g, ost[1], p.count("osB"))

    blk = p.block()
    if True:
        @blk.tensor
        def _(te):
            p.reset_counts()
            schedule({"t": te, "v": DryEng(), "sc": DryEng(), "s": DryEng(),
                      "g": DryEng()})

        @blk.vector
        def _(ve):
            p.reset_counts()
            p.emit_consts(ve)
            schedule({"t": DryEng(), "v": ve, "sc": DryEng(), "s": DryEng(),
                      "g": DryEng()})

        @blk.scalar
        def _(sce):
            p.reset_counts()
            schedule({"t": DryEng(), "v": DryEng(), "sc": sce, "s": DryEng(),
                      "g": DryEng()})

        @blk.sync
        def _(se):
            p.reset_counts()
            schedule({"t": DryEng(), "v": DryEng(), "sc": DryEng(), "s": se,
                      "g": DryEng()})

        @blk.gpsimd
        def _(ge):
            p.reset_counts()
            schedule({"t": DryEng(), "v": DryEng(), "sc": DryEng(),
                      "s": DryEng(), "g": ge})

    p.close()
    return p.nc



def _consts(cfg):
    NHC = cfg.NHC
    ecols = np.zeros((P, NHC * NHC), np.float32)
    for h in range(NHC):
        ecols[:, h * NHC + h] = 1.0
    rowsel = np.zeros((NHC, NHC * P), np.float32)
    for h in range(NHC):
        rowsel[h, h * P:(h + 1) * P] = 1.0
    stril = -np.tril(np.ones((P, P), np.float32), -1)
    triu = np.triu(np.ones((P, P), np.float32))
    identb = np.eye(P, dtype=np.float32).astype(BFD)
    identf = np.eye(P, dtype=np.float32)
    rsel12 = np.zeros((16, 16 * P), np.float32)
    for h in range(16):
        rsel12[h, h * P:(h + 1) * P] = 1.0
    return ecols, rowsel, stril, triu, identb, identf, rsel12


def kernel(L0_attn_hidden_states, L0_attn_k_cache=None, L0_attn_v_cache=None,
           L0_attn_cos=None, L0_attn_sin=None, L0_attn_Wq=None, L0_attn_Wk=None,
           L0_attn_Wv=None, L0_attn_Wo=None, L0_attn_Wg=None, L0_attn_Wb=None,
           L0_attn_conv_wq=None, L0_attn_conv_wk=None, L0_attn_conv_wv=None,
           L0_attn_norm_w=None, L0_ffn_classifier_w=None,
           L0_ffn_e_score_correction_bias=None, L0_ffn_gate_w=None,
           L0_ffn_up_w=None, L0_ffn_down_w=None):
    cfg = CFG()
    x = np.asarray(L0_attn_hidden_states)
    B = 2
    T, D, TQ, NHC = cfg.T, cfg.D, cfg.TQ, cfg.NHC
    HC = cfg.HCOLS
    ecols, rowsel, stril, triu, identb, identf, rsel12 = _consts(cfg)

    def core_bq(c):
        return c // 4, c % 4

    nc1 = build_l1(cfg)
    in1 = []
    for c in range(8):
        b, q = core_bq(c)
        in1.append({"x_tok": np.ascontiguousarray(x[b, q * TQ:(q + 1) * TQ, :])})
    r1 = run_bass_kernel_spmd(nc1, in1, core_ids=list(range(8)))
    xn_tok = [r1.results[c]["xn_tok"] for c in range(8)]

    xn_fm = []
    for b in range(B):
        full = np.concatenate([xn_tok[4 * b + q] for q in range(4)], axis=0)
        xn_fm.append(np.ascontiguousarray(full.T))

    nc2 = build_l2(cfg)
    in2 = []
    Wq = np.asarray(L0_attn_Wq)
    Wk = np.asarray(L0_attn_Wk)
    Wv = np.asarray(L0_attn_Wv)
    Wg = np.asarray(L0_attn_Wg)
    Wb = np.asarray(L0_attn_Wb)
    cwq = np.asarray(L0_attn_conv_wq)
    cwk = np.asarray(L0_attn_conv_wk)
    cwv = np.asarray(L0_attn_conv_wv)
    for c in range(8):
        b, q = core_bq(c)
        cols = slice(q * HC, (q + 1) * HC)
        in2.append({
            "xn_fm": xn_fm[b],
            "wq": np.ascontiguousarray(Wq[:, cols]),
            "wk": np.ascontiguousarray(Wk[:, cols]),
            "wv": np.ascontiguousarray(Wv[:, cols]),
            "wg": np.ascontiguousarray(Wg[:, cols]),
            "wb": np.ascontiguousarray(Wb[:, q * NHC:(q + 1) * NHC]),
            "cwq": np.ascontiguousarray(cwq[cols, :]),
            "cwk": np.ascontiguousarray(cwk[cols, :]),
            "cwv": np.ascontiguousarray(cwv[cols, :]),
            "e_cols": ecols, "rowsel": rowsel,
        })
    r2 = run_bass_kernel_spmd(nc2, in2, core_ids=list(range(8)))

    nc3 = build_l3(cfg)
    in3 = []
    nwrep = np.ascontiguousarray(
        np.broadcast_to(np.asarray(L0_attn_norm_w, np.float32)[None, :], (P, P)))
    for c in range(8):
        o2 = r2.results[c]
        in3.append({
            "kh_d": o2["kh_d"], "qc_d": o2["qc_d"], "g_d": o2["g_d"],
            "kt": np.ascontiguousarray(np.asarray(o2["kh_d"]).T),
            "vt": np.ascontiguousarray(np.asarray(o2["vc_d"]).T),
            "beta_t": np.ascontiguousarray(np.asarray(o2["beta_d"]).T),
            "kap_t": np.ascontiguousarray(np.asarray(o2["kapq_d"]).T),
            "strilneg": stril, "triuincl": triu, "ident": identb,
            "normw_rep": nwrep,
        })
    r3 = run_bass_kernel_spmd(nc3, in3, core_ids=list(range(8)))

    o_fm = []
    for b in range(B):
        o_fm.append(np.concatenate([r3.results[4 * b + q]["og_d"]
                                    for q in range(4)], axis=0))

    nc4 = build_l4(cfg)
    wcls = np.ascontiguousarray(np.asarray(L0_ffn_classifier_w).T)
    ebias_rep = np.ascontiguousarray(np.broadcast_to(
        np.asarray(L0_ffn_e_score_correction_bias, np.float32)[None, :], (P, 12)))
    gateT = np.ascontiguousarray(np.asarray(L0_ffn_gate_w).transpose(0, 2, 1))
    upT = np.ascontiguousarray(np.asarray(L0_ffn_up_w).transpose(0, 2, 1))
    downT = np.ascontiguousarray(np.asarray(L0_ffn_down_w).transpose(0, 2, 1))
    wo = np.asarray(L0_attn_Wo)
    in4 = []
    for c in range(8):
        b, q = core_bq(c)
        tsl = slice(q * TQ, (q + 1) * TQ)
        in4.append({
            "xn_fm": np.ascontiguousarray(xn_fm[b][:, tsl]),
            "o_fm": np.ascontiguousarray(o_fm[b][:, tsl]),
            "x_fm": np.ascontiguousarray(x[b].T[:, tsl]),
            "wcls": wcls, "ebias_rep": ebias_rep, "rsel12": rsel12,
            "identf": identf, "gateT": gateT, "upT": upT, "downT": downT,
            "wo": wo,
        })
    r4 = run_bass_kernel_spmd(nc4, in4, core_ids=list(range(8)))

    out = np.empty((B, T, D), np.float32)
    for b in range(B):
        full = np.concatenate([r4.results[4 * b + q]["out_fm"]
                               for q in range(4)], axis=1)
        out[b] = full.T
    return out
